# version 01 (final)
# baseline (speedup 1.0000x reference)
import os
import numpy as np
import ml_dtypes
import concourse.bass as bass
import concourse.mybir as mybir
from concourse.bass_core_rust import IndirectOffsetOnAxis
from concourse.bass_utils import run_bass_kernel_spmd
from contextlib import ExitStack

F32 = mybir.dt.float32
BF16 = mybir.dt.bfloat16
U32 = mybir.dt.uint32
AF = mybir.ActivationFunctionType
ALU = mybir.AluOpType
AX = mybir.AxisListType.X

D = 1024
DC = 8
TPC = 32
K = 512
NE = 8
ROWS = NE * K
OORV = 3_000_000.0

MK_STAGE = int(os.environ.get("MK_STAGE", "9"))
TAPS = os.environ.get("MK_TAPS", "0") in ("1", "2")
TAPS2 = os.environ.get("MK_TAPS", "0") == "2"

ENGINES = ("sync", "tensor", "vector", "scalar", "gpsimd")


class Step:
    __slots__ = ("sid", "eng", "fn", "deps", "hard", "inc", "total", "tag", "flush",
                 "semkey")

    def __init__(self, sid, eng, fn, deps, hard, inc, tag, flush):
        self.sid, self.eng, self.fn, self.deps, self.hard, self.inc, self.tag = (
            sid, eng, fn, deps, hard, inc, tag)
        self.total = 0
        self.flush = flush
        self.semkey = "gpsimd_dma" if (eng == "gpsimd" and inc is not None) else eng


class St:
    def __init__(self, prog, step):
        self.prog, self.step, self.total = prog, step, 0

    def dma(self, instr, amt=16):
        instr.then_inc(self.prog.sem[self.step.semkey], amt)
        self.total += amt
        return instr


class Prog:
    def __init__(self, nc, ctx):
        self.nc = nc
        self.steps = []
        self.sem = {e: ctx.enter_context(nc.semaphore(f"se_{e}")) for e in ENGINES}
        self.sem["gpsimd_dma"] = ctx.enter_context(nc.semaphore("sd_gpsimd"))
        self.sem["coll"] = ctx.enter_context(nc.semaphore("sc_coll"))
        self._barrier_pending = {}
        self.flush_fn = None

    def step(self, eng, fn, deps=(), hard=(), inc=None, tag="", flush=False,
             semkey=None):
        sid = len(self.steps)
        dl = [d for d in deps if d is not None]
        hl = [d for d in hard if d is not None]
        if eng in self._barrier_pending:
            dl.extend(self._barrier_pending.pop(eng))
        for d in dl + hl:
            assert 0 <= d < sid, f"bad dep {d} for step {sid} {tag}"
        s = Step(sid, eng, fn, dl, hl, inc, tag, flush)
        if semkey is not None:
            s.semkey = semkey
        self.steps.append(s)
        return sid

    def barrier(self):
        last = {}
        for s in self.steps:
            last[s.eng] = s.sid
        dep_list = list(last.values())
        self._barrier_pending = {e: list(dep_list) for e in ENGINES}

    def emit(self, blk):
        needed = set()
        by_eng = {e: [] for e in ENGINES}
        for s in self.steps:
            by_eng[s.eng].append(s)
            for d in s.deps:
                if self.steps[d].eng != s.eng:
                    needed.add(d)
            for d in s.hard:
                needed.add(d)
        cum = {e: 0 for e in ENGINES}
        cum["gpsimd_dma"] = 0
        cum["coll"] = 0
        after = {}
        for s in self.steps:
            s.total = s.inc if s.inc is not None else (1 if s.sid in needed else 0)
            cum[s.semkey] += s.total
            after[s.sid] = cum[s.semkey]
        prev_of = {}
        for e in ENGINES:
            pr = None
            for s in by_eng[e]:
                prev_of[s.sid] = pr
                pr = s.sid
        prog = self

        def make_section(eng_name):
            def section(eng):
                hwm = {e: 0 for e in ENGINES}
                hwm["gpsimd_dma"] = 0
                hwm["coll"] = 0
                for s in by_eng[eng_name]:
                    for d in sorted(set(s.deps) | set(s.hard)):
                        ds = prog.steps[d]
                        if ds.eng == eng_name and d not in s.hard:
                            continue
                        val = after[d]
                        if val > hwm[ds.semkey]:
                            eng.wait_ge(prog.sem[ds.semkey], val)
                            hwm[ds.semkey] = val
                    eng.nop()
                    if s.inc is not None:
                        st = St(prog, s)
                        s.fn(eng, st)
                        assert st.total == s.total, (
                            f"dma step {s.sid}({s.tag}): declared {s.total} != {st.total}")
                    else:
                        ret = s.fn(eng, None)
                        if s.flush:
                            eng.drain()
                            ret = prog.flush_fn(eng)
                        if s.total:
                            assert ret is not None, f"step {s.sid}({s.tag}) returned None"
                            ret.then_inc(prog.sem[s.semkey], 1)
            return section

        blk.sync(make_section("sync"))
        blk.tensor(make_section("tensor"))
        blk.vector(make_section("vector"))
        blk.scalar(make_section("scalar"))
        blk.gpsimd(make_section("gpsimd"))


C_I1024, C_NHALF, C_EPS5, C_EPS12, C_GC1, C_GC2 = 0, 1, 2, 3, 4, 5


def IBCOL(e, i):
    return 128 * (e // 4) + 32 * (e % 4) + i


def build_nc():
    nc = bass.Bass("TRN2", target_bir_lowering=False, detect_race_conditions=False)
    ap = lambda t: t.ap()

    u_t = ap(nc.dram_tensor("u_t", [TPC * 128, D], F32, kind="ExternalInput"))
    u_halo = ap(nc.dram_tensor("u_halo", [32, D], F32, kind="ExternalInput"))
    WBd = ap(nc.dram_tensor("WB", [D, D], F32, kind="ExternalInput"))
    WBfd = ap(nc.dram_tensor("WBf", [D, D], F32, kind="ExternalInput"))
    w1d = ap(nc.dram_tensor("w1", [1, D], F32, kind="ExternalInput"))
    w2d = ap(nc.dram_tensor("w2", [1, D], F32, kind="ExternalInput"))
    bd = ap(nc.dram_tensor("bsc", [1, 1], F32, kind="ExternalInput"))
    nwd = ap(nc.dram_tensor("nwv", [1, D], F32, kind="ExternalInput"))
    nbd = ap(nc.dram_tensor("nbv", [1, D], F32, kind="ExternalInput"))
    wgd = ap(nc.dram_tensor("wg", [D, NE], F32, kind="ExternalInput"))
    rcd = ap(nc.dram_tensor("rcoef", [D, 2], F32, kind="ExternalInput"))
    iwd = ap(nc.dram_tensor("iw", [NE, D, 4096], F32, kind="ExternalInput"))
    ibd = ap(nc.dram_tensor("ib", [NE, 4096], F32, kind="ExternalInput"))
    owd = ap(nc.dram_tensor("ow", [NE, 4096, D], F32, kind="ExternalInput"))
    obd = ap(nc.dram_tensor("ob", [NE, D], F32, kind="ExternalInput"))
    riwd = ap(nc.dram_tensor("riw", [D, 4096], F32, kind="ExternalInput"))
    ribd = ap(nc.dram_tensor("rib", [1, 4096], F32, kind="ExternalInput"))
    rowd = ap(nc.dram_tensor("row_w", [4096, D], F32, kind="ExternalInput"))
    robd = ap(nc.dram_tensor("rob", [1, D], F32, kind="ExternalInput"))
    identbd = ap(nc.dram_tensor("ident_b", [128, 128], BF16, kind="ExternalInput"))
    identfd = ap(nc.dram_tensor("ident_f", [128, 128], F32, kind="ExternalInput"))
    triubd = ap(nc.dram_tensor("triu_b", [128, 128], BF16, kind="ExternalInput"))
    striud = ap(nc.dram_tensor("striu32", [32, 32], F32, kind="ExternalInput"))
    id32d = ap(nc.dram_tensor("ident32", [32, 32], F32, kind="ExternalInput"))
    ones1d = ap(nc.dram_tensor("ones1x128", [1, 128], F32, kind="ExternalInput"))
    ones128bd = ap(nc.dram_tensor("ones128_b", [128, 1], BF16, kind="ExternalInput"))
    iotad = ap(nc.dram_tensor("iota_in", [128, NE], F32, kind="ExternalInput"))
    constsd = ap(nc.dram_tensor("consts_in", [128, 16], F32, kind="ExternalInput"))
    mfd = ap(nc.dram_tensor("mfirst", [128, 2], F32, kind="ExternalInput"))
    mld = ap(nc.dram_tensor("mlast", [128, 2], F32, kind="ExternalInput"))
    oord = ap(nc.dram_tensor("oor_in", [128, 1], F32, kind="ExternalInput"))
    opvd = ap(nc.dram_tensor("oneprev", [8, 2], F32, kind="ExternalInput"))

    out_d = ap(nc.dram_tensor("out_d", [TPC * 128, D], F32, kind="ExternalOutput"))
    h1_d = ap(nc.dram_tensor("h1_d", [TPC * 128, D], F32, kind="Internal"))
    disp_d = ap(nc.dram_tensor("disp_d", [ROWS, D], BF16, kind="Internal"))
    eout_d = ap(nc.dram_tensor("eout_d", [ROWS, D], BF16, kind="Internal"))
    rh_d = ap(nc.dram_tensor("rh_d", [32, 8, 128, 512], BF16, kind="Internal"))
    res_d = ap(nc.dram_tensor("res_d", [TPC * 128, D], BF16, kind="Internal"))
    cci_t = nc.dram_tensor("cc_in", [1, 256], F32, kind="Internal")
    cci = ap(cci_t)
    cco_t = nc.dram_tensor("cc_out", [8, 256], F32, kind="Internal", addr_space="Shared")
    cco = ap(cco_t)
    bo_d = ap(nc.dram_tensor("bo_d", [2, 256], F32, kind="Internal"))

    taps = {}
    if TAPS:
        taps["t_ao"] = ap(nc.dram_tensor("t_ao", [TPC * 128, D], F32, kind="ExternalOutput"))
        if TAPS2:
            for nm in ("t_v", "t_xf", "t_xb", "t_y", "t_h1"):
                taps[nm] = ap(nc.dram_tensor(nm, [TPC * 128, D], F32, kind="ExternalOutput"))
            taps["t_gt"] = ap(nc.dram_tensor("t_gt", [TPC * 128, 32], F32, kind="ExternalOutput"))
            taps["t_cnt"] = ap(nc.dram_tensor("t_cnt", [128, 256], F32, kind="ExternalOutput"))
            taps["t_lg"] = ap(nc.dram_tensor("t_lg", [TPC * 128, 16], F32, kind="ExternalOutput"))
            taps["t_gt2"] = ap(nc.dram_tensor("t_gt2", [TPC * 128, 32], F32, kind="ExternalOutput"))
            taps["t_mkb"] = ap(nc.dram_tensor("t_mkb", [128, 256], BF16, kind="ExternalOutput"))
            taps["t_mkf"] = ap(nc.dram_tensor("t_mkf", [128, 256], F32, kind="ExternalOutput"))
            taps["t_ccr"] = ap(nc.dram_tensor("t_ccr", [4, 256], F32, kind="ExternalOutput"))
            taps["t_ps"] = ap(nc.dram_tensor("t_ps", [128, 32], F32, kind="ExternalOutput"))
        taps["t_g"] = ap(nc.dram_tensor("t_g", [128, 192], F32, kind="ExternalOutput"))
        taps["t_idx"] = ap(nc.dram_tensor("t_idx", [128, 32], U32, kind="ExternalOutput"))
        taps["t_cc"] = ap(nc.dram_tensor("t_cc", [8, 256], F32, kind="ExternalOutput"))
        taps["t_bo"] = ap(nc.dram_tensor("t_bo", [2, 256], F32, kind="ExternalOutput"))

    with ExitStack() as top:
        E = top.enter_context

        mask_f = E(nc.sbuf_tensor("mask_f", [128, TPC * 8], F32))
        mask_b = E(nc.sbuf_tensor("mask_b", [128, TPC * 8], BF16))
        g1b = E(nc.sbuf_tensor("g1b", [128, TPC], F32))
        c0b = E(nc.sbuf_tensor("c0b", [128, TPC], F32))
        c1b = E(nc.sbuf_tensor("c1b", [128, TPC], F32))
        r2b = E(nc.sbuf_tensor("r2b", [128, TPC], F32))
        nmr2b = E(nc.sbuf_tensor("nmr2b", [128, TPC], F32))
        idxb = E(nc.sbuf_tensor("idxb", [128, TPC], U32))
        cc_row = E(nc.sbuf_tensor("cc_row", [1, 256], F32))
        bo_r = E(nc.sbuf_tensor("bo_r", [1, 512], F32))
        cnt_all = E(nc.sbuf_tensor("cnt_all", [128, 256], F32))
        CO = E(nc.sbuf_tensor("CO", [128, 16], F32))
        iot = E(nc.sbuf_tensor("iot", [128, NE], F32))
        identb = E(nc.sbuf_tensor("identb", [128, 128], BF16))
        identf = E(nc.sbuf_tensor("identf", [128, 128], F32))
        triub = E(nc.sbuf_tensor("triub", [128, 128], BF16))
        striu = E(nc.sbuf_tensor("striu", [32, 32], F32))
        id32 = E(nc.sbuf_tensor("id32", [32, 32], F32))
        ones1 = E(nc.sbuf_tensor("ones1", [1, 128], F32))
        ones128b = E(nc.sbuf_tensor("ones128b", [128, 1], BF16))
        mf_s = E(nc.sbuf_tensor("mf_s", [128, 2], F32))
        ml_s = E(nc.sbuf_tensor("ml_s", [128, 2], F32))
        oor_s = E(nc.sbuf_tensor("oor_s", [128, 1], F32))
        opv = E(nc.sbuf_tensor("opv", [8, 2], F32))
        bcol = E(nc.sbuf_tensor("bcol", [128, 1], F32))
        ribc = E(nc.sbuf_tensor("ribc", [128, 32], F32))
        ibc = E(nc.sbuf_tensor("ibc", [128, 32 * NE], F32))
        halo_fm = E(nc.sbuf_tensor("halo_fm", [128, DC * 32], BF16))
        wcols = E(nc.sbuf_tensor("wcols", [128, DC * 4], BF16))
        wgs = E(nc.sbuf_tensor("wgs", [128, DC * NE], BF16))
        rcs = E(nc.sbuf_tensor("rcs", [128, DC * 2], BF16))
        cs32 = E(nc.sbuf_tensor("cs32", [2, 256], F32))
        G_sb = E(nc.sbuf_tensor("G_sb", [8, 256], F32))
        own32 = E(nc.sbuf_tensor("own32", [32, 8], F32))
        cs32s = E(nc.sbuf_tensor("cs32s", [32, 8], F32))
        t232 = E(nc.sbuf_tensor("t232", [32, 8], F32))
        b32 = E(nc.sbuf_tensor("b32", [32, 8], F32))
        o32 = E(nc.sbuf_tensor("o32", [32, 8], F32))

        fl_dst = E(nc.sbuf_tensor("fl_dst", [128, 128], F32))
        p = Prog(nc, top)
        p.flush_fn = lambda e: e.tensor_copy(out=fl_dst[:], in_=identf[:])
        blk = E(nc.Block())

        co = lambda i: CO[:, i:i + 1]
        co32 = lambda i: CO[0:32, i:i + 1]

        def ln_chain(pfx, src_fn, base, epsc, m, extra_sq_deps=(), extra_apply_deps=(),
                     out_fn=None, region=None, save_rn=False):
            rb = region
            sSa = p.step("vector", lambda e, _: e.reduce_sum(
                st8[:, rb:rb + 1], src_fn(), axis=AX),
                deps=list(extra_sq_deps), tag=f"{pfx}sa_{m}")
            sQ = p.step("scalar", lambda e, _: e.activation(
                out=dmp[:], in_=src_fn(), func=AF.Square, accum_out=st8[:, rb + 2:rb + 3]),
                deps=list(extra_sq_deps), tag=f"{pfx}q_{m}")
            sV = p.step("vector", lambda e, _: (
                e.tensor_scalar(st8[:, rb + 1:rb + 2], st8[:, rb:rb + 1], 1.0 / D,
                                None, op0=ALU.mult),
                e.tensor_scalar(st8[:, rb + 3:rb + 4], st8[:, rb + 2:rb + 3], 1.0 / D,
                                None, op0=ALU.mult),
                e.nop(),
                e.tensor_mul(st8[:, rb + 4:rb + 5], st8[:, rb + 1:rb + 2],
                             st8[:, rb + 1:rb + 2]),
                e.nop(),
                e.tensor_sub(st8[:, rb + 5:rb + 6], st8[:, rb + 3:rb + 4],
                             st8[:, rb + 4:rb + 5]))[-1],
                deps=[sQ], hard=[sSa], tag=f"{pfx}v_{m}")
            sR = p.step("scalar", lambda e, _: (
                e.activation(out=st8[:, rb + 6:rb + 7], in_=st8[:, rb + 5:rb + 6],
                             func=AF.Ln, bias=epsc, scale=1.0),
                e.nop(),
                e.activation(out=st8[:, rb + 6:rb + 7], in_=st8[:, rb + 6:rb + 7],
                             func=AF.Exp, bias=0.0, scale=co(C_NHALF)))[-1],
                deps=[sV], tag=f"{pfx}r_{m}")

            def f_n(e, _):
                e.tensor_copy(out=st8[:, rb + 8:rb + 9], in_=st8[:, rb + 6:rb + 7])
                e.scalar_tensor_tensor(st8[:, rb + 7:rb + 8], st8[:, rb + 1:rb + 2],
                                       -1.0, st8[:, rb + 6:rb + 7],
                                       op0=ALU.mult, op1=ALU.mult)
                r = e.nop()
                if save_rn:
                    e.tensor_copy(out=r2b[:, m:m + 1], in_=st8[:, rb + 8:rb + 9])
                    e.nop()
                    r = e.tensor_copy(out=nmr2b[:, m:m + 1], in_=st8[:, rb + 7:rb + 8])
                else:
                    e.nop()
                    r = e.tensor_copy(out=st8[:, rb + 9:rb + 10], in_=st8[:, rb + 7:rb + 8])
                return r
            sN = p.step("vector", f_n, deps=[sR], tag=f"{pfx}n_{m}")
            sA = p.step("scalar", lambda e, _: e.activation(
                out=out_fn(), in_=src_fn(), func=AF.Identity, bias=st8[:, rb + 7:rb + 8],
                scale=st8[:, rb + 8:rb + 9]),
                deps=[sN] + list(extra_apply_deps), tag=f"{pfx}a_{m}")
            return sA

        pao = ExitStack()
        ao_fm = pao.enter_context(nc.sbuf_tensor("ao_fm", [128, DC * 4096], BF16))
        zero_b = pao.enter_context(nc.sbuf_tensor("zero_b", [128, D], BF16))
        p1 = ExitStack()
        WBT = p1.enter_context(nc.sbuf_tensor("WBT", [128, DC * 1024], BF16))
        WBfT = p1.enter_context(nc.sbuf_tensor("WBfT", [128, DC * 1024], BF16))
        nwb = p1.enter_context(nc.sbuf_tensor("nwb", [128, D], F32))
        nbb = p1.enter_context(nc.sbuf_tensor("nbb", [128, D], F32))
        pWB = ExitStack()
        WB_sb = pWB.enter_context(nc.sbuf_tensor("WB_sb", [128, DC * 1024], BF16))
        WBf_sb = pWB.enter_context(nc.sbuf_tensor("WBf_sb", [128, DC * 1024], BF16))
        w1r = pWB.enter_context(nc.sbuf_tensor("w1r", [1, D], BF16))
        w2r = pWB.enter_context(nc.sbuf_tensor("w2r", [1, D], BF16))
        w1c = pWB.enter_context(nc.sbuf_tensor("w1c", [128, DC], BF16))
        w2c = pWB.enter_context(nc.sbuf_tensor("w2c", [128, DC], BF16))
        nwr = pWB.enter_context(nc.sbuf_tensor("nwr", [1, D], F32))
        nbr = pWB.enter_context(nc.sbuf_tensor("nbr", [1, D], F32))
        bsb = pWB.enter_context(nc.sbuf_tensor("bsb", [1, 1], F32))
        uh_s = pWB.enter_context(nc.sbuf_tensor("uh_s", [32, D], F32))
        uh_v = pWB.enter_context(nc.sbuf_tensor("uh_v", [32, D], F32))
        uh_st = pWB.enter_context(nc.sbuf_tensor("uh_st", [32, 10], F32))
        vh_b = pWB.enter_context(nc.sbuf_tensor("vh_b", [32, D], BF16))
        ibs2 = pWB.enter_context(nc.sbuf_tensor("ibs2", [128, 256], F32))
        ribs32 = pWB.enter_context(nc.sbuf_tensor("ribs32", [32, 128], F32))
        ps_b = pWB.enter_context(nc.psum_tensor("ps_b", [128, 1024], BF16))
        ps_f = pWB.enter_context(nc.psum_tensor("ps_f", [128, 512], F32))
        ps_f2 = pWB.enter_context(nc.psum_tensor("ps_f2", [128, 1024], F32))

        def d_consts(eng, st):
            for src, dst in ((identbd, identb), (identfd, identf), (triubd, triub),
                             (striud, striu), (id32d, id32), (ones1d, ones1),
                             (ones128bd, ones128b), (iotad, iot), (constsd, CO),
                             (mfd, mf_s), (mld, ml_s), (oord, oor_s), (opvd, opv),
                             (bd, bsb), (nwd, nwr), (nbd, nbr), (u_halo, uh_s)):
                st.dma(eng.dma_start(out=dst[:], in_=src))
            ib_r = ibd.rearrange("e (i p) -> (e i) p", p=128)
            st.dma(eng.dma_start(out=ibs2[:, 0:128], in_=ib_r[0:128, :]))
            st.dma(eng.dma_start(out=ibs2[:, 128:256], in_=ib_r[128:256, :]))
            st.dma(eng.dma_start(out=ribs32[:], in_=ribd.rearrange("o (i p) -> (o i) p", p=128)))
        sC = p.step("sync", d_consts, inc=16 * 20, tag="consts")

        breg = {}

        def d_wcast(eng, st):
            breg["r"] = eng.to_reg(ROWS - 1)
            st.dma(eng.dma_start(out=WB_sb[:].rearrange("p (c d) -> p c d", d=1024), in_=WBd.rearrange("(c p) d -> p c d", p=128)))
            st.dma(eng.dma_start(out=WBf_sb[:].rearrange("p (c d) -> p c d", d=1024), in_=WBfd.rearrange("(c p) d -> p c d", p=128)))
            st.dma(eng.dma_start(out=w1r[:], in_=w1d))
            st.dma(eng.dma_start(out=w2r[:], in_=w2d))
            st.dma(eng.dma_start(out=wgs[:].rearrange("p (c e) -> p c e", e=8), in_=wgd.rearrange("(c p) e -> p c e", p=128)))
            st.dma(eng.dma_start(out=rcs[:].rearrange("p (c e) -> p c e", e=2), in_=rcd.rearrange("(c p) e -> p c e", p=128)))
        sW = p.step("gpsimd", d_wcast, inc=16 * 6, tag="wcast")

        sZm = p.step("vector", lambda e, _: e.memset(zero_b[:], 0.0), tag="zero_b")

        def d_zero(eng, st):
            for j in range(ROWS // 128):
                st.dma(eng.dma_start(out=disp_d[128 * j:128 * (j + 1), :], in_=zero_b[:]))
        sZ = p.step("gpsimd", d_zero, deps=[sZm], inc=16 * (ROWS // 128), tag="zero_disp")

        sH1_ = p.step("vector", lambda e, _: e.reduce_sum(
            uh_st[:, 0:1], uh_s[:], axis=AX), deps=[sC], tag="hstat")
        sH2_ = p.step("scalar", lambda e, _: e.activation(
            out=uh_v[:], in_=uh_s[:], func=AF.Square, accum_out=uh_st[:, 2:3]),
            deps=[sC], tag="hsq")
        sH3_ = p.step("vector", lambda e, _: (
            e.tensor_scalar(uh_st[:, 1:2], uh_st[:, 0:1], 1.0 / D, None, op0=ALU.mult),
            e.tensor_scalar(uh_st[:, 3:4], uh_st[:, 2:3], 1.0 / D, None, op0=ALU.mult),
            e.nop(),
            e.tensor_mul(uh_st[:, 4:5], uh_st[:, 1:2], uh_st[:, 1:2]),
            e.nop(),
            e.tensor_sub(uh_st[:, 5:6], uh_st[:, 3:4], uh_st[:, 4:5]))[-1],
            deps=[sH2_], hard=[sH1_], tag="hvar")
        sH4_ = p.step("scalar", lambda e, _: (
            e.activation(out=uh_st[:, 6:7], in_=uh_st[:, 5:6], func=AF.Ln,
                         bias=co32(C_EPS5), scale=1.0),
            e.nop(),
            e.activation(out=uh_st[:, 6:7], in_=uh_st[:, 6:7], func=AF.Exp,
                         bias=0.0, scale=co32(C_NHALF)))[-1],
            deps=[sH3_], tag="hr")
        def f_h5(e, _):
            e.tensor_copy(out=uh_st[:, 8:9], in_=uh_st[:, 6:7])
            return e.scalar_tensor_tensor(
                uh_st[:, 7:8], uh_st[:, 1:2], -1.0, uh_st[:, 6:7],
                op0=ALU.mult, op1=ALU.mult)
        sH5_ = p.step("vector", f_h5, deps=[sH4_], tag="hnmr")
        sH6_ = p.step("scalar", lambda e, _: e.activation(
            out=uh_v[:], in_=uh_s[:], func=AF.Identity, bias=uh_st[:, 7:8], scale=uh_st[:, 8:9]),
            deps=[sH5_], tag="happly")
        sH7_ = p.step("vector", lambda e, _: e.tensor_copy(out=vh_b[:], in_=uh_v[:]),
                      deps=[sH6_], tag="hcast")
        sH8_ = p.step("tensor", lambda e, _: [
            e.transpose(ps_b[:, 32 * dc:32 * (dc + 1)],
                        vh_b[0:32, 128 * dc:128 * (dc + 1)], identb[0:32, 0:32])
            for dc in range(DC)][-1], deps=[sH7_, sC], tag="htr")
        sH9_ = p.step("scalar", lambda e, _: e.copy(out=halo_fm[:], in_=ps_b[:, 0:DC * 32]),
                      deps=[sH8_], tag="hfm")

        prev_cp = sH9_
        for w_i, (src, dst) in enumerate(((WB_sb, WBT), (WBf_sb, WBfT))):
            for dc in range(DC):
                def f_tr(e, _, src=src, dc=dc):
                    last = None
                    for ec in range(DC):
                        last = e.transpose(ps_b[:, 128 * ec:128 * (ec + 1)],
                                           src[:, 1024 * ec + 128 * dc:1024 * ec + 128 * (dc + 1)],
                                           identb[:])
                    return last
                stp = p.step("tensor", f_tr, deps=[sW, sC, prev_cp], tag=f"wtr{w_i}_{dc}")
                prev_cp = p.step("scalar", lambda e, _, dst=dst, dc=dc: e.copy(
                    out=dst[:, 1024 * dc:1024 * (dc + 1)], in_=ps_b[:]),
                    deps=[stp], tag=f"wcp{w_i}_{dc}")

        for (wr, wc, tg) in ((w1r, w1c, "w1c"), (w2r, w2c, "w2c")):
            stp = p.step("tensor", lambda e, _, wr=wr: [
                e.transpose(ps_b[:, 2 * dc:2 * dc + 1], wr[0:1, 128 * dc:128 * (dc + 1)],
                            identb[0:1, 0:1]) for dc in range(DC)][-1],
                deps=[sW, prev_cp], tag=tg)
            prev_cp = p.step("scalar", lambda e, _, wc=wc: e.copy(
                out=wc[:], in_=ps_b[:, 0:2 * DC:2]), deps=[stp], tag=tg + "cp")

        for (src, wc, col0, tg) in ((WB_sb, w1c, 0, "wb1"), (WBf_sb, w2c, 2, "wb2")):
            def f_wb(e, _, src=src, wc=wc):
                last = None
                for dc in range(DC):
                    for ec in range(DC):
                        last = e.matmul(ps_f[:, dc:dc + 1],
                                        src[:, 1024 * ec + 128 * dc:1024 * ec + 128 * (dc + 1)],
                                        wc[:, ec:ec + 1], start=(ec == 0), stop=(ec == 7))
                return last
            stp = p.step("tensor", f_wb, deps=[prev_cp], tag=tg)
            prev_cp = p.step("vector", lambda e, _, col0=col0: e.tensor_copy(
                out=wcols[:, col0::4], in_=ps_f[:, 0:DC]), deps=[stp], tag=tg + "cp")
        p.step("vector", lambda e, _: e.tensor_copy(out=wcols[:, 1::4], in_=w1c[:]),
               deps=[prev_cp], tag="wpk1")
        sWPK = p.step("vector", lambda e, _: e.tensor_copy(out=wcols[:, 3::4], in_=w2c[:]),
                      deps=[prev_cp], tag="wpk2")

        sB1 = p.step("tensor", lambda e, _: e.matmul(
            ps_f2[:, 0:1], ones1[:], bsb[:], start=True, stop=True),
            deps=[sC, sWPK], tag="bcolmm")
        sB2 = p.step("vector", lambda e, _: e.tensor_copy(out=bcol[:], in_=ps_f2[:, 0:1]),
                     deps=[sB1], tag="bcolcp")
        prevb = sB2
        for (srcr, dstb, tg) in ((nwr, nwb, "nwb"), (nbr, nbb, "nbb")):
            smm = p.step("tensor", lambda e, _, srcr=srcr: (
                e.matmul(ps_f2[:, 0:512], ones1[:], srcr[0:1, 0:512], start=True, stop=True),
                e.matmul(ps_f2[:, 512:1024], ones1[:], srcr[0:1, 512:1024],
                         start=True, stop=True))[-1],
                deps=[sC, prevb], tag=tg + "mm")
            prevb = p.step("vector", lambda e, _, dstb=dstb: e.tensor_copy(
                out=dstb[:], in_=ps_f2[:]), deps=[smm], tag=tg + "cp")

        srt = p.step("tensor", lambda e, _: e.transpose(
            ps_f[:, 0:32], ribs32[:], identf[0:32, 0:32]),
            deps=[sC, prevb], tag="ribtr")
        sribc = p.step("vector", lambda e, _: e.tensor_copy(out=ribc[:], in_=ps_f[:, 0:32]),
                       deps=[srt], tag="ribcp")
        prev_ib = sribc
        for g in range(2):
            stp = p.step("tensor", lambda e, _, g=g: e.transpose(
                ps_f[:, 0:128], ibs2[:, 128 * g:128 * (g + 1)], identf[:]),
                deps=[sC, prev_ib], tag=f"ibtr{g}")
            prev_ib = p.step("vector", lambda e, _, g=g: e.tensor_copy(
                out=ibc[:, 128 * g:128 * (g + 1)], in_=ps_f[:, 0:128]),
                deps=[stp], tag=f"ibcp{g}")
        p0_last = [sH9_, sWPK, prevb, sribc, prev_ib, sB2]

        pWB.close()

        u_rg = [p1.enter_context(nc.sbuf_tensor(f"u{r}", [128, D], F32)) for r in range(3)]
        vt_rg = [p1.enter_context(nc.sbuf_tensor(f"vt{r}", [128, D], F32)) for r in range(2)]
        vb_rg = [p1.enter_context(nc.sbuf_tensor(f"vb{r}", [128, D], BF16)) for r in range(2)]
        vfm_rg = [p1.enter_context(nc.sbuf_tensor(f"vfm{r}", [128, DC * 132], BF16))
                  for r in range(2)]
        y_rg = [p1.enter_context(nc.sbuf_tensor(f"y{r}", [128, D], F32)) for r in range(2)]
        xf_s = p1.enter_context(nc.sbuf_tensor("xf_s", [128, D], F32))
        xb_s = p1.enter_context(nc.sbuf_tensor("xb_s", [128, D], F32))
        h1_rg = [p1.enter_context(nc.sbuf_tensor(f"h1_{r}", [128, D], F32)) for r in range(2)]
        dmp = p1.enter_context(nc.sbuf_tensor("dmp", [128, D], F32))
        nrm = p1.enter_context(nc.sbuf_tensor("nrm", [128, D], F32))
        aon = p1.enter_context(nc.sbuf_tensor("aon", [128, D], F32))
        aon1 = p1.enter_context(nc.sbuf_tensor("aon1", [128, D], F32))
        aot = p1.enter_context(nc.sbuf_tensor("aot", [128, D], F32))
        aob_rg = [p1.enter_context(nc.sbuf_tensor(f"aob{r}", [128, D], BF16)) for r in range(2)]
        st8 = p1.enter_context(nc.sbuf_tensor("st8", [128, 20], F32))
        gt8 = p1.enter_context(nc.sbuf_tensor("gt8", [128, 32], F32))
        e8 = p1.enter_context(nc.sbuf_tensor("e8", [128, NE], F32))
        lgst = p1.enter_context(nc.sbuf_tensor("lgst", [128, 16], F32))
        e8b = p1.enter_context(nc.sbuf_tensor("e8b", [128, NE], F32))
        ec2 = p1.enter_context(nc.sbuf_tensor("ec2", [128, 2], F32))
        pvt = p1.enter_context(nc.psum_tensor("pvt", [128, 1024], BF16))
        pxf = p1.enter_context(nc.psum_tensor("pxf", [128, 1024], F32))
        pxb = p1.enter_context(nc.psum_tensor("pxb", [128, 1024], F32))
        pat = p1.enter_context(nc.psum_tensor("pat", [128, 1024], BF16))
        psm = p1.enter_context(nc.psum_tensor("psm", [128, 128], F32))
        psm2 = p1.enter_context(nc.psum_tensor("psm2", [128, 64], F32))

        L = {k: {} for k in ("uld", "vtok", "vbf", "vtr", "vfm", "mm", "evx", "evb",
                             "y", "lny", "h1", "h1d", "ao", "pm", "aob", "atr",
                             "aofm", "gat", "tot", "ccw", "coef", "tap", "cnv",
                             "tap3", "tap4")}

        for m in range(TPC):
            q = m % 2
            rb = 10 * q
            gb = 16 * q
            sU = p.step("sync", lambda e, st, m=m: st.dma(e.dma_start(
                out=u_rg[m % 3][:], in_=u_t[128 * m:128 * (m + 1), :])),
                deps=[L["h1"].get(m - 3)] + (p0_last if m < 3 else []),
                inc=16, tag=f"uld{m}")
            L["uld"][m] = sU

            src_u = (lambda m=m: u_rg[m % 3][:])
            sVT = ln_chain("ln1", lambda m=m: u_rg[m % 3][:], None, co(C_EPS5), m,
                           extra_sq_deps=[sU, L["ao"].get(m - 2)],
                           extra_apply_deps=[L["y"].get(m - 2)],
                           out_fn=lambda m=m: vt_rg[m % 2][:], region=rb)
            L["vtok"][m] = sVT
            sVB = p.step("vector", lambda e, _, m=m: e.tensor_copy(
                out=vb_rg[m % 2][:], in_=vt_rg[m % 2][:]),
                deps=[sVT, L["vtr"].get(m - 2)], tag=f"vb_{m}")
            L["vbf"][m] = sVB

            def f_tr(e, _, m=m):
                last = None
                for dc in range(DC):
                    last = e.transpose(pvt[:, 128 * dc:128 * (dc + 1)],
                                       vb_rg[m % 2][:, 128 * dc:128 * (dc + 1)], identb[:])
                return last
            sTR = p.step("tensor", f_tr, deps=[sVB, L["vfm"].get(m - 1)],
                         tag=f"tr_{m}")
            L["vtr"][m] = sTR

            def f_vfm(e, _, m=m):
                e.tensor_copy(
                    out=vfm_rg[m % 2][:].rearrange("p (c w) -> p c w", w=132)[:, :, 0:128],
                    in_=pvt[:].rearrange("p (c w) -> p c w", w=128))
                return e.tensor_copy(
                    out=vfm_rg[m % 2][:].rearrange("p (c w) -> p c w", w=132)[:, :, 128:129],
                    in_=halo_fm[:].rearrange("p (c w) -> p c w", w=32)[:, :, m:m + 1])
            sVF = p.step("vector", f_vfm, deps=[sTR, L["mm"].get(m - 1), sH9_],
                         tag=f"vfm_{m}")
            L["vfm"][m] = sVF

            def f_mm(e, _, m=m):
                vfm = vfm_rg[m % 2]
                last = None
                for dc in range(DC):
                    lT_f = vfm[:, 132 * dc:132 * dc + 128]
                    for h in range(2):
                        e.matmul(pxf[:, 512 * h:512 * (h + 1)], lT_f,
                                 WBT[:, 1024 * dc + 512 * h:1024 * dc + 512 * (h + 1)],
                                 start=(dc == 0), stop=(dc == 7))
                    e.matmul(psm[:, 0:2], lT_f, wcols[:, 4 * dc:4 * dc + 2],
                             start=(dc == 0), stop=(dc == 7))
                for dc in range(DC):
                    lT_b = vfm[:, 132 * dc + 1:132 * dc + 129]
                    for h in range(2):
                        e.matmul(pxb[:, 512 * h:512 * (h + 1)], lT_b,
                                 WBfT[:, 1024 * dc + 512 * h:1024 * dc + 512 * (h + 1)],
                                 start=(dc == 0), stop=(dc == 7))
                    last = e.matmul(psm[:, 2:4], lT_b, wcols[:, 4 * dc + 2:4 * dc + 4],
                                    start=(dc == 0), stop=(dc == 7))
                return last
            sMM = p.step("tensor", f_mm,
                         deps=[sVF, L["y"].get(m - 1),
                               L.get("tap2", {}).get(m - 1) if TAPS2 else None],
                         tag=f"mm_{m}")
            L["mm"][m] = sMM
            sEX = p.step("scalar", lambda e, _: e.copy(out=xf_s[:], in_=pxf[:]),
                         deps=[sMM], tag=f"ex_{m}")
            L["evx"][m] = sEX
            sEB = p.step("scalar", lambda e, _: e.copy(out=xb_s[:], in_=pxb[:]),
                         deps=[sMM], tag=f"eb_{m}")
            L["evb"][m] = sEB

            sDE = p.step("vector", lambda e, _, gb=gb: e.tensor_copy(
                out=gt8[:, gb + 3:gb + 7], in_=psm[:, 0:4]),
                deps=[sMM, L["tap4"].get(m - 2) if TAPS2 else None], tag=f"de_{m}")
            bdeps = [sEX, sEB, sDE]
            prevbl = None
            if m % 8 == 0:
                def f_bl1(e, _, m=m, gb=gb):
                    e.tensor_scalar(xf_s[:], xf_s[:], mf_s[:, 1:2], None, op0=ALU.mult)
                    e.tensor_scalar(gt8[:, gb + 3:gb + 4], gt8[:, gb + 3:gb + 4],
                                    mf_s[:, 1:2], None, op0=ALU.mult)
                    e.nop()
                    e.scalar_tensor_tensor(xf_s[:], vt_rg[m % 2][:], mf_s[:, 0:1], xf_s[:],
                                           op0=ALU.mult, op1=ALU.add)
                    return e.scalar_tensor_tensor(gt8[:, gb + 3:gb + 4], gt8[:, gb + 4:gb + 5],
                                                  mf_s[:, 0:1], gt8[:, gb + 3:gb + 4],
                                                  op0=ALU.mult, op1=ALU.add)
                prevbl = p.step("vector", f_bl1, deps=bdeps, tag=f"blf_{m}")
            if m % 8 == 7:
                def f_bl2(e, _, m=m, gb=gb):
                    e.tensor_scalar(xb_s[:], xb_s[:], ml_s[:, 1:2], None, op0=ALU.mult)
                    e.tensor_scalar(gt8[:, gb + 5:gb + 6], gt8[:, gb + 5:gb + 6],
                                    ml_s[:, 1:2], None, op0=ALU.mult)
                    e.nop()
                    e.scalar_tensor_tensor(xb_s[:], vt_rg[m % 2][:], ml_s[:, 0:1], xb_s[:],
                                           op0=ALU.mult, op1=ALU.add)
                    return e.scalar_tensor_tensor(gt8[:, gb + 5:gb + 6], gt8[:, gb + 6:gb + 7],
                                                  ml_s[:, 0:1], gt8[:, gb + 5:gb + 6],
                                                  op0=ALU.mult, op1=ALU.add)
                prevbl = p.step("vector", f_bl2, deps=bdeps + [prevbl], tag=f"bll_{m}")

            sGD = p.step("vector", lambda e, _, gb=gb: e.tensor_add(
                gt8[:, gb + 0:gb + 1], gt8[:, gb + 3:gb + 4], gt8[:, gb + 5:gb + 6]),
                deps=bdeps, hard=[prevbl, sDE], tag=f"gd_{m}")
            sGM = p.step("scalar", lambda e, _, gb=gb: e.activation(
                out=gt8[:, gb + 1:gb + 2], in_=gt8[:, gb + 0:gb + 1], func=AF.Sigmoid,
                bias=bcol[:, 0:1], scale=1.0), deps=[sGD], tag=f"gm_{m}")

            def f_y(e, _, m=m, gb=gb):
                e.tensor_scalar(gt8[:, gb + 2:gb + 3], gt8[:, gb + 1:gb + 2], -1.0, 1.0,
                                op0=ALU.mult, op1=ALU.add)
                e.tensor_add(y_rg[m % 2][:], xf_s[:], xb_s[:])
                e.nop()
                e.tensor_scalar(y_rg[m % 2][:], y_rg[m % 2][:], gt8[:, gb + 1:gb + 2],
                                None, op0=ALU.mult)
                e.nop()
                return e.scalar_tensor_tensor(y_rg[m % 2][:], vt_rg[m % 2][:],
                                              gt8[:, gb + 2:gb + 3], y_rg[m % 2][:],
                                              op0=ALU.mult, op1=ALU.add)
            sY = p.step("vector", f_y, deps=[sGM, sEX, sEB, L["lny"].get(m - 2)],
                        hard=[prevbl] if prevbl is not None else [], tag=f"y_{m}")
            L["y"][m] = sY
            if TAPS2:
                def f_dtap(e, st, m=m):
                    st.dma(e.dma_start(out=taps["t_v"][128 * m:128 * (m + 1), :],
                                       in_=vt_rg[m % 2][:]))
                    st.dma(e.dma_start(out=taps["t_xf"][128 * m:128 * (m + 1), :],
                                       in_=xf_s[:]))
                    st.dma(e.dma_start(out=taps["t_xb"][128 * m:128 * (m + 1), :],
                                       in_=xb_s[:]))
                    st.dma(e.dma_start(out=taps["t_y"][128 * m:128 * (m + 1), :],
                                       in_=y_rg[m % 2][:]))
                    st.dma(e.dma_start(out=taps["t_gt"][128 * m:128 * (m + 1), :],
                                       in_=gt8[:]))
                L["tap2"] = getattr(L, "tap2", {}) if False else L.setdefault("tap2", {})
                L["tap2"][m] = p.step("sync", f_dtap, deps=[sY], inc=80, tag=f"dtap{m}")

            sLY = ln_chain("lny", lambda m=m: y_rg[m % 2][:], None, co(C_EPS5), m,
                           extra_sq_deps=[sY, L["ao"].get(m - 2)],
                           extra_apply_deps=[L["ao"].get(m - 1)],
                           out_fn=lambda: nrm[:], region=rb)
            L["lny"][m] = sLY
            sH1 = p.step("vector", lambda e, _, m=m: e.tensor_add(
                h1_rg[m % 2][:], u_rg[m % 3][:], nrm[:]),
                deps=[sLY, L["h1d"].get(m - 2), L["ao"].get(m - 2)], tag=f"h1_{m}")
            L["h1"][m] = sH1
            def f_h1d(e, st, m=m):
                st.dma(e.dma_start(out=h1_d[128 * m:128 * (m + 1), :], in_=h1_rg[m % 2][:]))
                if TAPS2:
                    st.dma(e.dma_start(out=taps["t_h1"][128 * m:128 * (m + 1), :],
                                       in_=h1_rg[m % 2][:]))
            sHD = p.step("sync", f_h1d, deps=[sH1], inc=32 if TAPS2 else 16, tag=f"h1d_{m}")
            L["h1d"][m] = sHD

            sL2 = ln_chain("ln2", lambda m=m: h1_rg[m % 2][:], None, co(C_EPS5), m,
                           extra_sq_deps=[sH1, L["ao"].get(m - 2)],
                           extra_apply_deps=[L["ao"].get(m - 1)],
                           out_fn=lambda: nrm[:], region=rb, save_rn=True)
            sAN = ln_chain("lnI", lambda: nrm[:], None, co(C_EPS12), m,
                           extra_sq_deps=[sL2, L["ao"].get(m - 2)],
                           extra_apply_deps=[L["pm"].get(m - 1)],
                           out_fn=lambda: aon[:], region=rb)
            sPM = p.step("gpsimd", lambda e, _: e.tensor_mul(aon1[:], aon[:], nwb[:]),
                         deps=[sAN, L["ao"].get(m - 1), L["tap"].get(m - 1)],
                         tag=f"pm_{m}")
            L["pm"][m] = sPM
            sAO = p.step("vector", lambda e, _: e.tensor_add(aot[:], aon1[:], nbb[:]),
                         deps=[sPM], tag=f"ao_{m}")
            L["ao"][m] = sAO
            sAB = p.step("vector", lambda e, _, m=m: e.tensor_copy(
                out=aob_rg[m % 2][:], in_=aot[:]),
                deps=[sAO, L["atr"].get(m - 2)], tag=f"ab_{m}")
            L["aob"][m] = sAB
            if TAPS:
                L["tap"][m] = p.step("sync", lambda e, st, m=m: st.dma(e.dma_start(
                    out=taps["t_ao"][128 * m:128 * (m + 1), :], in_=aot[:])),
                    deps=[sAO], inc=16, tag=f"tapao{m}")

            def f_atr(e, _, m=m):
                e.matmul(psm[:, 40:48], triub[:], triub[:, 0:8], start=True, stop=True)
                last = None
                for dc in range(DC):
                    last = e.transpose(pat[:, 128 * dc:128 * (dc + 1)],
                                       aob_rg[m % 2][:, 128 * dc:128 * (dc + 1)], identb[:])
                return last
            sAT = p.step("tensor", f_atr, deps=[sAB, L["aofm"].get(m - 1)], tag=f"at_{m}")
            L["atr"][m] = sAT
            sAF = p.step("vector", lambda e, _, m=m: e.tensor_copy(
                out=ao_fm[:].rearrange("p (c w) -> p c w", w=4096)[:, :, 128 * m:128 * (m + 1)],
                in_=pat[:].rearrange("p (c w) -> p c w", w=128)),
                deps=[sAT], tag=f"aofm_{m}")
            L["aofm"][m] = sAF

            def f_lg(e, _, m=m):
                last = None
                for dc in range(DC):
                    lT = ao_fm[:, 4096 * dc + 128 * m:4096 * dc + 128 * (m + 1)]
                    e.matmul(psm2[:, 0:8], lT, wgs[:, 8 * dc:8 * (dc + 1)],
                             start=(dc == 0), stop=(dc == 7))
                for dc in range(DC):
                    lT = ao_fm[:, 4096 * dc + 128 * m:4096 * dc + 128 * (m + 1)]
                    last = e.matmul(psm2[:, 8:10], lT, rcs[:, 2 * dc:2 * (dc + 1)],
                                    start=(dc == 0), stop=(dc == 7))
                return last
            sLG = p.step("tensor", f_lg, deps=[sAF, L["gat"].get(m - 1),
                                               L["coef"].get(m - 1),
                                               L["tap3"].get(m - 1) if TAPS2 else None],
                         tag=f"lg_{m}")
            if TAPS2:
                sT3c = p.step("vector", lambda e, _: e.tensor_copy(
                    out=lgst[:], in_=psm2[:, 0:16]), deps=[sLG], tag=f"taplgc{m}")
                L["tap3"][m] = p.step("sync", lambda e, st, m=m: st.dma(e.dma_start(
                    out=taps["t_lg"][128 * m:128 * (m + 1), :], in_=lgst[:])),
                    deps=[sT3c], inc=16, tag=f"taplg{m}")

            sG1a = p.step("vector", lambda e, _, gb=gb: e.reduce_max(
                gt8[:, gb + 5:gb + 6], psm2[:, 0:8], axis=AX),
                deps=[sLG], tag=f"g1a_{m}")
            sG1 = p.step("vector", lambda e, _, gb=gb: e.tensor_scalar(
                gt8[:, gb + 6:gb + 7], gt8[:, gb + 5:gb + 6], -1.0, None, op0=ALU.mult),
                hard=[sG1a], tag=f"g1_{m}")
            sGE = p.step("scalar", lambda e, _, gb=gb: e.activation(
                out=e8[:], in_=psm2[:, 0:8], func=AF.Exp, bias=gt8[:, gb + 6:gb + 7],
                scale=1.0), deps=[sG1], tag=f"ge_{m}")

            sG2a = p.step("vector", lambda e, _, gb=gb: e.reduce_sum(
                gt8[:, gb + 7:gb + 8], e8[:], axis=AX), deps=[sGE], tag=f"g2a_{m}")
            sG2b = p.step("vector", lambda e, _, gb=gb: e.reciprocal(
                gt8[:, gb + 8:gb + 9], gt8[:, gb + 7:gb + 8]),
                hard=[sG2a], tag=f"g2b_{m}")
            def f_g2c(e, _, m=m, gb=gb):
                e.tensor_copy(out=g1b[:, m:m + 1], in_=gt8[:, gb + 8:gb + 9])
                return e.tensor_scalar(e8b[:], psm2[:, 0:8], gt8[:, gb + 5:gb + 6],
                                       None, op0=ALU.is_equal)
            sG2c0 = p.step("vector", f_g2c, hard=[sG2b], tag=f"g2c0_{m}")
            sG2c = p.step("vector", lambda e, _: e.scalar_tensor_tensor(
                e8[:], e8b[:], -1024.0, iot[:], op0=ALU.mult, op1=ALU.add),
                hard=[sG2c0], tag=f"g2c_{m}")
            sG2d = p.step("vector", lambda e, _, gb=gb: e.tensor_reduce(
                gt8[:, gb + 9:gb + 10], e8[:], AX, ALU.min),
                hard=[sG2c], tag=f"g2d_{m}")
            sG2e0 = p.step("vector", lambda e, _, gb=gb: e.tensor_scalar(
                gt8[:, gb + 10:gb + 11], gt8[:, gb + 9:gb + 10], 1024.0,
                None, op0=ALU.add), hard=[sG2d], tag=f"g2e0_{m}")

            def f_g2e(e, _, m=m, gb=gb):
                e.tensor_scalar(mask_f[:, 8 * m:8 * (m + 1)], iot[:],
                                gt8[:, gb + 10:gb + 11], None, op0=ALU.is_equal)
                return e.tensor_scalar(mask_b[:, 8 * m:8 * (m + 1)], iot[:],
                                       gt8[:, gb + 10:gb + 11], None,
                                       op0=ALU.is_equal)
            sG2 = p.step("vector", f_g2e, hard=[sG2e0], tag=f"g2_{m}", flush=True)
            L["gat"][m] = sG2
            sTT = None
            sC1a = p.step("vector", lambda e, _, gb=gb: e.reduce_max(
                gt8[:, gb + 11:gb + 12], psm2[:, 8:10], axis=AX),
                deps=[sLG], tag=f"cf1a_{m}")
            sC1 = p.step("vector", lambda e, _, gb=gb: e.tensor_scalar(
                gt8[:, gb + 12:gb + 13], gt8[:, gb + 11:gb + 12], -1.0, None,
                op0=ALU.mult), hard=[sC1a], tag=f"cf1_{m}")
            sC2 = p.step("scalar", lambda e, _, gb=gb: e.activation(
                out=ec2[:], in_=psm2[:, 8:10], func=AF.Exp, bias=gt8[:, gb + 12:gb + 13],
                scale=1.0), deps=[sC1], tag=f"cf2_{m}")
            sTT = p.step("tensor", lambda e, _, m=m: e.matmul(
                psm[:, 32:40], triub[:], mask_b[:, 8 * m:8 * (m + 1)],
                start=True, stop=True),
                deps=[sG2], tag=f"tt_{m}")
            L["tot"][m] = sTT

            sC3a = p.step("vector", lambda e, _, gb=gb: e.reduce_sum(
                gt8[:, gb + 13:gb + 14], ec2[:], axis=AX), deps=[sC2], tag=f"cf3a_{m}")
            sC3b = p.step("vector", lambda e, _, gb=gb: e.reciprocal(
                gt8[:, gb + 14:gb + 15], gt8[:, gb + 13:gb + 14]),
                hard=[sC3a], tag=f"cf3b_{m}")
            def f_cf3(e, _, m=m, gb=gb):
                e.tensor_scalar(c0b[:, m:m + 1], ec2[:, 0:1], gt8[:, gb + 14:gb + 15],
                                None, op0=ALU.mult)
                return e.tensor_scalar(c1b[:, m:m + 1], ec2[:, 1:2],
                                       gt8[:, gb + 14:gb + 15], None, op0=ALU.mult)
            sC3 = p.step("vector", f_cf3, hard=[sC3b], tag=f"cf3_{m}")
            L["coef"][m] = sC3
            if TAPS2:
                L["tap4"][m] = p.step("sync", lambda e, st, m=m: st.dma(e.dma_start(
                    out=taps["t_gt2"][128 * m:128 * (m + 1), :], in_=gt8[:])),
                    deps=[sC3, L["gat"][m]], inc=16, tag=f"tapgt2_{m}")

        def f_tta(e, _):
            return e.matmul(pxf[:, 0:256], triub[:], mask_b[:, 0:256],
                            start=True, stop=True)
        sTTA = p.step("tensor", f_tta,
                      deps=[L["gat"][TPC - 1], L["evx"][TPC - 1]], tag="cntmm")
        sCVA = p.step("vector", lambda e, _: e.tensor_copy(
            out=cnt_all[:], in_=pxf[:, 0:256]), deps=[sTTA],
            tag="cntcp", flush=True)
        sCWA = p.step("gpsimd", lambda e, st: st.dma(e.dma_start(
            out=cci, in_=cnt_all[127:128, :])), deps=[sCVA], inc=16, tag="cntdma")
        if TAPS2:
            def f_tapcnt(e, st):
                st.dma(e.dma_start(out=taps["t_cnt"], in_=cnt_all[:]))
                st.dma(e.dma_start(out=taps["t_mkb"], in_=mask_b[:]))
                st.dma(e.dma_start(out=taps["t_mkf"], in_=mask_f[:]))
            p.step("sync", f_tapcnt, deps=[sCVA], inc=48, tag="tapcnt")

        p1_last = [L["h1d"][TPC - 1], L["aofm"][TPC - 1], L["coef"][TPC - 1],
                   sCWA, L["gat"][TPC - 1]]

        if TAPS:
            def f_tapg(e, st):
                st.dma(e.dma_start(out=taps["t_g"][:, 0:32], in_=g1b[:]))
                st.dma(e.dma_start(out=taps["t_g"][:, 32:64], in_=c0b[:]))
                st.dma(e.dma_start(out=taps["t_g"][:, 64:96], in_=c1b[:]))
                st.dma(e.dma_start(out=taps["t_g"][:, 96:128], in_=r2b[:]))
                st.dma(e.dma_start(out=taps["t_g"][:, 128:160], in_=nmr2b[:]))
                st.dma(e.dma_start(out=taps["t_g"][:, 160:192], in_=mask_f[:, 0:32]))
            p.step("sync", f_tapg, deps=p1_last, inc=16 * 6, tag="tapg")

        p.barrier()
        p1.close()

        p23 = ExitStack()
        riw_sb = p23.enter_context(nc.sbuf_tensor("riw_sb", [128, DC * 4096], BF16))
        gsA = [p23.enter_context(nc.sbuf_tensor(f"gsA{r}", [128, 512], F32)) for r in range(2)]
        gsB = [p23.enter_context(nc.sbuf_tensor(f"gsB{r}", [128, 512], F32)) for r in range(2)]
        rhb = [p23.enter_context(nc.sbuf_tensor(f"rhb{r}", [128, 512], BF16)) for r in range(2)]
        aotk = [p23.enter_context(nc.sbuf_tensor(f"aotk{r}", [128, D], BF16)) for r in range(2)]
        g16 = p23.enter_context(nc.sbuf_tensor("g16", [128, 16], F32))
        bb_sb = p23.enter_context(nc.sbuf_tensor("bb_sb", [128, 256], F32))
        own_sb = p23.enter_context(nc.sbuf_tensor("own_sb", [128, 256], F32))
        gsc = p23.enter_context(nc.sbuf_tensor("gsc", [128, 8], F32))
        prh = [p23.enter_context(nc.psum_tensor(f"prh{r}", [128, 512], F32)) for r in range(4)]
        pat2 = p23.enter_context(nc.psum_tensor("pat2", [128, 1024], BF16))
        pb32 = p23.enter_context(nc.psum_tensor("pb32", [128, 512], F32))

        if TAPS2:
            def f_tapccr(e, st):
                st.dma(e.dma_start(out=taps["t_ccr"][1:2, :], in_=cci))
            p.step("sync", f_tapccr, deps=[sCWA], inc=16, tag="tapccr")
        sX2 = p.step("gpsimd", lambda e, st: st.dma(e.collective_compute(
            "AllGather", ALU.bypass, replica_groups=[list(range(8))],
            ins=[cci_t.ap().opt()], outs=[cco_t.ap().opt()]), amt=1),
            deps=[sCWA], inc=1, tag="cc_ag", semkey="coll")

        def f_ccdn(e, st):
            st.dma(e.dma_start(out=G_sb[:], in_=cco))
            st.dma(e.dma_start(out=own32[:], in_=cci.rearrange("o (m e) -> (o m) e", e=8)))
        sX3 = p.step("gpsimd", f_ccdn, deps=[sX2], inc=32, tag="cc_dn")
        if TAPS:
            p.step("sync", lambda e, st: st.dma(e.dma_start(out=taps["t_cc"], in_=G_sb[:])),
                   deps=[sX3], inc=16, tag="tapcc")

        sRW = p.step("gpsimd", lambda e, st: st.dma(e.dma_start(
            out=riw_sb[:].rearrange("p (c i) -> p c i", i=4096), in_=riwd.rearrange("(c p) i -> p c i", p=128))),
            deps=[sX3], inc=16, tag="riw_ld")

        g_done4 = {}
        rh_dma = {}
        it = 0
        for i in range(32):
            for tb in range(8):
                slot = it % 4

                def f_rmm(e, _, i=i, tb=tb, slot=slot):
                    last = None
                    for h in range(DC):
                        last = e.matmul(
                            prh[slot][:],
                            riw_sb[:, 4096 * h + 128 * i:4096 * h + 128 * (i + 1)],
                            ao_fm[:, 4096 * h + 512 * tb:4096 * h + 512 * (tb + 1)],
                            start=(h == 0), stop=(h == 7))
                    return last
                sM = p.step("tensor", f_rmm, deps=[sRW, g_done4.get(it - 4)],
                            tag=f"rmm{it}")

                def f_ga(e, _, i=i, it=it, slot=slot):
                    e.activation(out=gsA[it % 2][:], in_=prh[slot][:], func=AF.Square,
                                 bias=ribc[:, i:i + 1], scale=1.0)
                    e.nop()
                    return e.activation(out=gsB[it % 2][:], in_=gsA[it % 2][:],
                                        func=AF.Identity, bias=co(C_GC2), scale=co(C_GC1))
                sA_ = p.step("scalar", f_ga, deps=[sM, g_done4.get(it - 2)], tag=f"ga{it}")

                def f_gv(e, _, i=i, it=it, slot=slot):
                    return e.scalar_tensor_tensor(gsA[it % 2][:], prh[slot][:],
                                                  ribc[:, i:i + 1], gsB[it % 2][:],
                                                  op0=ALU.add, op1=ALU.mult)
                sV_ = p.step("vector", f_gv, deps=[sA_], tag=f"gv{it}")
                sS_ = p.step("scalar", lambda e, _, it=it: e.activation(
                    out=gsB[it % 2][:], in_=gsA[it % 2][:], func=AF.Sigmoid),
                    deps=[sV_], tag=f"gs{it}")

                def f_gr(e, _, i=i, it=it, slot=slot):
                    return e.scalar_tensor_tensor(rhb[it % 2][:], prh[slot][:],
                                                  ribc[:, i:i + 1], gsB[it % 2][:],
                                                  op0=ALU.add, op1=ALU.mult)
                sR_ = p.step("vector", f_gr, deps=[sS_, rh_dma.get(it - 2)], tag=f"gr{it}")
                g_done4[it] = sR_
                sD_ = p.step("sync", lambda e, st, i=i, tb=tb, it=it: st.dma(e.dma_start(
                    out=rh_d[i, tb], in_=rhb[it % 2][:])), deps=[sR_], inc=16,
                    tag=f"rhd{it}")
                rh_dma[it] = sD_
                it += 1
        p4a_last = rh_dma[it - 1]

        sX4 = p.step("tensor", lambda e, _: e.matmul(
            pb32[0:2, 0:256], opv[:], G_sb[:], start=True, stop=True),
            deps=[sX3], hard=[sM], tag="csmm")
        sX5 = p.step("vector", lambda e, _: e.tensor_copy(out=cs32[:], in_=pb32[0:2, 0:256]),
                     deps=[sX4], tag="cscp", flush=True)
        sX6 = p.step("sync", lambda e, st: st.dma(e.dma_start(out=bo_d, in_=cs32[:])),
                     deps=[sX5], inc=16, tag="bo_up")

        def f_bodn(e, st):
            st.dma(e.dma_start(out=cs32s[:], in_=bo_d[0:1, :].rearrange("o (m e) -> (o m) e", e=8)))
            st.dma(e.dma_start(out=t232[:], in_=bo_d[1:2, :].rearrange("o (m e) -> (o m) e", e=8)))
        sX7 = p.step("sync", f_bodn, deps=[sX6], inc=32, tag="bo_dn")

        def f_bmm(e, _):
            e.matmul(pb32[0:32, 0:8], striu[:], cs32s[:], start=True, stop=False)
            e.matmul(pb32[0:32, 0:8], id32[:], t232[:], start=False, stop=True)
            return e.matmul(pb32[0:32, 8:16], striu[:], own32[:], start=True, stop=True)
        sX8 = p.step("tensor", f_bmm, deps=[sX7, sX3, sX5], tag="bmm")

        def f_bcp(e, _):
            e.tensor_copy(out=b32[:], in_=pb32[0:32, 0:8])
            return e.tensor_copy(out=o32[:], in_=pb32[0:32, 8:16])
        sX9 = p.step("vector", f_bcp, deps=[sX8], tag="bcp", flush=True)

        def f_bup(e, st):
            st.dma(e.dma_start(out=bo_d[0:1, :].rearrange("o (m e) -> (o m) e", e=8), in_=b32[:]))
            st.dma(e.dma_start(out=bo_d[1:2, :].rearrange("o (m e) -> (o m) e", e=8), in_=o32[:]))
        sXa = p.step("sync", f_bup, deps=[sX9], inc=32, tag="b_up")

        def f_bdn(e, st):
            st.dma(e.dma_start(out=bo_r[0:1, 0:256], in_=bo_d[0:1, :]))
            st.dma(e.dma_start(out=bo_r[0:1, 256:512], in_=bo_d[1:2, :]))
        sXb = p.step("sync", f_bdn, deps=[sXa], inc=32, tag="b_dn")
        sBK = p.step("tensor", lambda e, _: e.matmul(
            pb32[:, 0:512], ones1[:], bo_r[0:1, 0:512], start=True, stop=True),
            deps=[sXb], tag="pbbk")

        def f_pbe(e, _):
            e.tensor_add(bb_sb[:], pb32[:, 0:256], cnt_all[:])
            return e.tensor_add(own_sb[:], pb32[:, 256:512], cnt_all[:])
        sBE = p.step("vector", f_pbe, deps=[sBK], tag="pbbe", flush=True)
        if TAPS:
            p.step("sync", lambda e, st: st.dma(e.dma_start(out=taps["t_bo"], in_=bo_d)),
                   deps=[sXa], inc=16, tag="tapbo")
        base_ready = sXb

        c_dv = {}
        c_cp = {}
        c_sc = {}
        for m in range(TPC):
            def f_cda(e, _, m=m):
                mk = mask_f[:, 8 * m:8 * (m + 1)]
                e.scalar_tensor_tensor(gsc[:], bb_sb[:, 8 * m:8 * (m + 1)], 1.0, mk,
                                       op0=ALU.mult, op1=ALU.mult,
                                       accum_out=g16[:, 0:1])
                e.scalar_tensor_tensor(gsc[:], own_sb[:, 8 * m:8 * (m + 1)], 1.0, mk,
                                       op0=ALU.mult, op1=ALU.mult,
                                       accum_out=g16[:, 1:2])
                return e.scalar_tensor_tensor(gsc[:], iot[:], 1.0, mk, op0=ALU.mult,
                                              op1=ALU.mult, accum_out=g16[:, 2:3])
            sCA = p.step("vector", f_cda, deps=[sBE, c_dv.get(m - 1)], tag=f"cda{m}")

            def f_cdv1(e, _, m=m):
                e.tensor_scalar(g16[:, 3:4], g16[:, 0:1], -1.0, None, op0=ALU.add)
                return e.tensor_scalar(g16[:, 4:5], g16[:, 1:2], -1.0, None,
                                       op0=ALU.add)
            sD1 = p.step("vector", f_cdv1, hard=[sCA], tag=f"cdv1_{m}")

            def f_cdv2(e, _, m=m):
                e.tensor_scalar(g16[:, 5:6], g16[:, 3:4], 4096.0, None, op0=ALU.is_lt)
                e.tensor_scalar(g16[:, 6:7], g16[:, 4:5], float(K), None, op0=ALU.is_lt)
                return e.scalar_tensor_tensor(g16[:, 8:9], g16[:, 2:3], float(K),
                                              g16[:, 4:5], op0=ALU.mult, op1=ALU.add)
            sD2 = p.step("vector", f_cdv2, hard=[sD1], tag=f"cdv2_{m}")
            sD3 = p.step("vector", lambda e, _: e.tensor_mul(
                g16[:, 7:8], g16[:, 5:6], g16[:, 6:7]), hard=[sD2], tag=f"cdv3_{m}")

            def f_cdv4(e, _, m=m):
                e.tensor_mul(g16[:, 9:10], g16[:, 8:9], g16[:, 7:8])
                return e.tensor_scalar(g16[:, 10:11], g16[:, 7:8], -1.0, 1.0,
                                       op0=ALU.mult, op1=ALU.add)
            sD4 = p.step("vector", f_cdv4, hard=[sD3], tag=f"cdv4_{m}")
            sD5 = p.step("vector", lambda e, _: e.tensor_mul(
                g16[:, 11:12], g16[:, 10:11], oor_s[:]), hard=[sD4], tag=f"cdv5_{m}")
            sD6 = p.step("vector", lambda e, _: e.tensor_add(
                g16[:, 12:13], g16[:, 9:10], g16[:, 11:12]), hard=[sD5],
                tag=f"cdv6_{m}")
            sCD = p.step("vector", lambda e, _, m=m: e.tensor_copy(
                out=idxb[:, m:m + 1], in_=g16[:, 12:13]), hard=[sD6],
                tag=f"cdv{m}", flush=True)
            c_dv[m] = sCD

            def f_ctr(e, _, m=m):
                e.matmul(pb32[:, 0:8], triub[:], triub[:, 0:8], start=True, stop=True)
                last = None
                for dc in range(DC):
                    last = e.transpose(pat2[:, 128 * dc:128 * (dc + 1)],
                                       ao_fm[:, 4096 * dc + 128 * m:4096 * dc + 128 * (m + 1)],
                                       identb[:])
                return last
            sCT = p.step("tensor", f_ctr, deps=[c_cp.get(m - 1), sBE], tag=f"ctr{m}")
            sCP = p.step("vector", lambda e, _, m=m: e.tensor_copy(
                out=aotk[m % 2][:], in_=pat2[:]), deps=[sCT, c_sc.get(m - 2)],
                tag=f"ccp{m}")
            c_cp[m] = sCP
            sSC = p.step("gpsimd", lambda e, st, m=m: st.dma(e.indirect_dma_start(
                out=disp_d, out_offset=IndirectOffsetOnAxis(idxb[:, m:m + 1], 0),
                in_=aotk[m % 2][:], in_offset=None,
                bounds_check=breg["r"], oob_is_err=False)),
                deps=[sCP, sCD], inc=16, tag=f"sc{m}")
            c_sc[m] = sSC
        scatter_last = c_sc[TPC - 1]

        if TAPS:
            p.step("sync", lambda e, st: st.dma(e.dma_start(out=taps["t_idx"], in_=idxb[:])),
                   deps=[c_dv[TPC - 1]], inc=16, tag="tapidx")

        if MK_STAGE < 4:
            p.barrier()
            p.step("vector", lambda e, _: e.nop() and None, deps=[scatter_last, p4a_last],
                   tag="fin")
            p.emit(blk)
            p23.close()
            pao.close()
            return nc, taps

        p.barrier()
        p23.close()
        pao.close()
        p5 = ExitStack()
        iw_sb = p5.enter_context(nc.sbuf_tensor("iw_sb", [128, DC * 4096], BF16))
        ow_sb = p5.enter_context(nc.sbuf_tensor("ow_sb", [128, 32 * 1024], BF16))
        xfm = p5.enter_context(nc.sbuf_tensor("xfm", [128, DC * K], BF16))
        hfm = p5.enter_context(nc.sbuf_tensor("hfm", [128, 32 * K], BF16))
        xtk = [p5.enter_context(nc.sbuf_tensor(f"xtk{r}", [128, D], BF16)) for r in range(2)]
        etk = [p5.enter_context(nc.sbuf_tensor(f"etk{r}", [128, D], BF16)) for r in range(2)]
        obr = p5.enter_context(nc.sbuf_tensor("obr", [1, D], F32))
        gA5 = [p5.enter_context(nc.sbuf_tensor(f"gA5{r}", [128, 512], F32)) for r in range(2)]
        gB5 = [p5.enter_context(nc.sbuf_tensor(f"gB5{r}", [128, 512], F32)) for r in range(2)]
        pxt = p5.enter_context(nc.psum_tensor("pxt", [128, 1024], BF16))
        ph5 = [p5.enter_context(nc.psum_tensor(f"ph5{r}", [128, 512], F32)) for r in range(3)]
        po5 = [p5.enter_context(nc.psum_tensor(f"po5{r}", [128, 1024], F32)) for r in range(2)]

        NT5 = K // 128
        TBS = ((0, 512),)
        w_iw = {}
        w_ow = {}
        w_iw[0] = p.step("gpsimd", lambda e, st: st.dma(e.dma_start(
            out=iw_sb[:].rearrange("p (c i) -> p c i", i=4096), in_=iwd[0].rearrange("(c p) i -> p c i", p=128))),
            inc=16, tag="iw0")
        w_ow[0] = p.step("gpsimd", lambda e, st: st.dma(e.dma_start(
            out=ow_sb[:].rearrange("p (c h) -> p c h", h=1024), in_=owd[0].rearrange("(c p) h -> p c h", p=128))),
            deps=[w_iw[0]], inc=16, tag="ow0")

        hfm_done = {}
        omm_last = {}
        e_eout = {}
        for ex in range(NE):
            sOB1 = p.step("sync", lambda e, st, ex=ex: st.dma(e.dma_start(
                out=obr[:], in_=obd[ex:ex + 1, :])), deps=[omm_last.get(ex - 1)],
                inc=16, tag=f"obr{ex}")
            x_cp = {}
            x_tr = {}
            for t5 in range(NT5):
                sXL = p.step("sync", lambda e, st, ex=ex, t5=t5: st.dma(e.dma_start(
                    out=xtk[t5 % 2][:],
                    in_=disp_d[K * ex + 128 * t5:K * ex + 128 * (t5 + 1), :])),
                    deps=[scatter_last, x_tr.get(t5 - 2), hfm_done.get(ex - 1)],
                    inc=16, tag=f"xl{ex}_{t5}")
                sXT = p.step("tensor", lambda e, _, t5=t5: (
                    e.matmul(ph5[0][:, 0:8], triub[:], triub[:, 0:8],
                             start=True, stop=True),
                    [e.transpose(pxt[:, 128 * dc:128 * (dc + 1)],
                                 xtk[t5 % 2][:, 128 * dc:128 * (dc + 1)], identb[:])
                     for dc in range(DC)][-1])[-1],
                    deps=[sXL, x_cp.get(t5 - 1)],
                    hard=[omm_last.get(ex - 1)] if t5 == 0 else [],
                    tag=f"xt{ex}_{t5}")
                x_tr[t5] = sXT
                sXC = p.step("vector", lambda e, _, t5=t5: e.tensor_copy(
                    out=xfm[:].rearrange("p (c w) -> p c w", w=K)[:, :, 128 * t5:128 * (t5 + 1)],
                    in_=pxt[:].rearrange("p (c w) -> p c w", w=128)),
                    deps=[sXT], tag=f"xc{ex}_{t5}")
                x_cp[t5] = sXC

            it5 = 0
            g_done5 = {}
            for i in range(32):
                for (off, tbw) in TBS:
                    slot = it5 % 3

                    def f_imm(e, _, i=i, off=off, tbw=tbw, slot=slot):
                        last = None
                        for h in range(DC):
                            last = e.matmul(
                                ph5[slot][:, 0:tbw],
                                iw_sb[:, 4096 * h + 128 * i:4096 * h + 128 * (i + 1)],
                                xfm[:, K * h + off:K * h + off + tbw],
                                start=(h == 0), stop=(h == 7))
                        return last
                    sIM = p.step("tensor", f_imm,
                                 deps=[w_iw[ex], x_cp[NT5 - 1], g_done5.get(it5 - 3),
                                       omm_last.get(ex - 1)],
                                 tag=f"im{ex}_{it5}")

                    def f_g5a(e, _, i=i, ex=ex, it5=it5, slot=slot, tbw=tbw):
                        e.activation(out=gA5[it5 % 2][:, 0:tbw], in_=ph5[slot][:, 0:tbw],
                                     func=AF.Square,
                                     bias=ibc[:, IBCOL(ex, i):IBCOL(ex, i) + 1], scale=1.0)
                        e.nop()
                        return e.activation(out=gB5[it5 % 2][:, 0:tbw],
                                            in_=gA5[it5 % 2][:, 0:tbw], func=AF.Identity,
                                            bias=co(C_GC2), scale=co(C_GC1))
                    sGA = p.step("scalar", f_g5a, deps=[sIM, g_done5.get(it5 - 2)],
                                 tag=f"g5a{ex}_{it5}")

                    def f_g5v(e, _, i=i, ex=ex, it5=it5, slot=slot, tbw=tbw):
                        return e.scalar_tensor_tensor(
                            gA5[it5 % 2][:, 0:tbw], ph5[slot][:, 0:tbw],
                            ibc[:, IBCOL(ex, i):IBCOL(ex, i) + 1], gB5[it5 % 2][:, 0:tbw],
                            op0=ALU.add, op1=ALU.mult)
                    sGV = p.step("vector", f_g5v, deps=[sGA], tag=f"g5v{ex}_{it5}")
                    sGS = p.step("scalar", lambda e, _, it5=it5, tbw=tbw: e.activation(
                        out=gB5[it5 % 2][:, 0:tbw], in_=gA5[it5 % 2][:, 0:tbw],
                        func=AF.Sigmoid), deps=[sGV], tag=f"g5s{ex}_{it5}")

                    def f_g5r(e, _, i=i, ex=ex, it5=it5, off=off, slot=slot, tbw=tbw):
                        return e.scalar_tensor_tensor(
                            hfm[:, K * i + off:K * i + off + tbw], ph5[slot][:, 0:tbw],
                            ibc[:, IBCOL(ex, i):IBCOL(ex, i) + 1], gB5[it5 % 2][:, 0:tbw],
                            op0=ALU.add, op1=ALU.mult)
                    sGR = p.step("vector", f_g5r, deps=[sGS], tag=f"g5r{ex}_{it5}")
                    g_done5[it5] = sGR
                    it5 += 1
            hfm_done[ex] = g_done5[it5 - 1]
            if ex + 1 < NE:
                w_iw[ex + 1] = p.step("gpsimd", lambda e, st, ex=ex: st.dma(e.dma_start(
                    out=iw_sb[:].rearrange("p (c i) -> p c i", i=4096), in_=iwd[ex + 1].rearrange("(c p) i -> p c i", p=128))),
                    deps=[hfm_done[ex]], inc=16, tag=f"iw{ex + 1}")

            o_dma = {}
            sOM = None
            for t5 in range(NT5):
                def f_omm(e, _, ex=ex, t5=t5):
                    for i in range(32):
                        for h in range(2):
                            e.matmul(
                                po5[t5 % 2][:, 512 * h:512 * (h + 1)],
                                hfm[:, K * i + 128 * t5:K * i + 128 * (t5 + 1)],
                                ow_sb[:, 1024 * i + 512 * h:1024 * i + 512 * (h + 1)],
                                start=(i == 0), stop=False)
                    e.matmul(po5[t5 % 2][:, 0:512], ones1[:], obr[0:1, 0:512],
                             start=False, stop=True)
                    return e.matmul(po5[t5 % 2][:, 512:1024], ones1[:],
                                    obr[0:1, 512:1024], start=False, stop=True)
                sOM = p.step("tensor", f_omm,
                             deps=[hfm_done[ex], w_ow[ex], sOB1, o_dma.get(t5 - 2)],
                             tag=f"om{ex}_{t5}")
                sOE = p.step("vector", lambda e, _, t5=t5: e.tensor_copy(
                    out=etk[t5 % 2][:], in_=po5[t5 % 2][:]),
                    deps=[sOM], tag=f"oe{ex}_{t5}")
                sOD = p.step("sync", lambda e, st, ex=ex, t5=t5: st.dma(e.dma_start(
                    out=eout_d[K * ex + 128 * t5:K * ex + 128 * (t5 + 1), :],
                    in_=etk[t5 % 2][:])), deps=[sOE], inc=16, tag=f"od{ex}_{t5}")
                o_dma[t5] = sOD
            omm_last[ex] = sOM
            e_eout[ex] = o_dma[NT5 - 1]
            if ex + 1 < NE:
                w_ow[ex + 1] = p.step("gpsimd", lambda e, st, ex=ex: st.dma(e.dma_start(
                    out=ow_sb[:].rearrange("p (c h) -> p c h", h=1024), in_=owd[ex + 1].rearrange("(c p) h -> p c h", p=128))),
                    deps=[sOM, w_iw[ex + 1]], inc=16, tag=f"ow{ex + 1}")
        eout_last = e_eout[NE - 1]

        p.barrier()
        p5.close()
        p4b = ExitStack()
        wo_sb = p4b.enter_context(nc.sbuf_tensor("wo_sb", [128, 32 * 1024], BF16))
        rst = [p4b.enter_context(nc.sbuf_tensor(f"rst{r}", [128, 32 * 512], BF16))
               for r in range(2)]
        robr = p4b.enter_context(nc.sbuf_tensor("robr", [1, D], F32))
        rtk = [p4b.enter_context(nc.sbuf_tensor(f"rtk{r}", [128, D], BF16)) for r in range(2)]
        pro = [p4b.enter_context(nc.psum_tensor(f"pro{r}", [128, 1024], F32)) for r in range(2)]

        sWO = p.step("gpsimd", lambda e, st: st.dma(e.dma_start(
            out=wo_sb[:].rearrange("p (c h) -> p c h", h=1024), in_=rowd.rearrange("(c p) h -> p c h", p=128))),
            inc=16, tag="wo_res")
        sRB1 = p.step("sync", lambda e, st: st.dma(e.dma_start(out=robr[:], in_=robd)),
                      inc=16, tag="robr")

        r_dma = {}
        for tg in range(8):
            def f_rld(e, st, tg=tg):
                for i in range(32):
                    st.dma(e.dma_start(out=rst[tg % 2][:, 512 * i:512 * (i + 1)],
                                       in_=rh_d[i, tg]))
            sRL = p.step("sync", f_rld,
                         deps=[p4a_last, r_dma.get((tg - 2, 3))], inc=16 * 32,
                         tag=f"rld{tg}")
            for tt in range(4):
                def f_bmm2(e, _, tg=tg, tt=tt):
                    for i in range(32):
                        for h in range(2):
                            e.matmul(
                                pro[tt % 2][:, 512 * h:512 * (h + 1)],
                                rst[tg % 2][:, 512 * i + 128 * tt:512 * i + 128 * (tt + 1)],
                                wo_sb[:, 1024 * i + 512 * h:1024 * i + 512 * (h + 1)],
                                start=(i == 0), stop=False)
                    e.matmul(pro[tt % 2][:, 0:512], ones1[:], robr[0:1, 0:512],
                             start=False, stop=True)
                    return e.matmul(pro[tt % 2][:, 512:1024], ones1[:],
                                    robr[0:1, 512:1024], start=False, stop=True)
                sBM = p.step("tensor", f_bmm2,
                             deps=[sRL, sWO, sRB1, r_dma.get((tg, tt - 2)),
                                   r_dma.get((tg - 1, tt + 2))],
                             tag=f"bmm{tg}_{tt}")
                sBE = p.step("vector", lambda e, _, tt=tt: e.tensor_copy(
                    out=rtk[tt % 2][:], in_=pro[tt % 2][:]),
                    deps=[sBM], tag=f"bev{tg}_{tt}")
                sBD = p.step("sync", lambda e, st, tg=tg, tt=tt: st.dma(e.dma_start(
                    out=res_d[512 * tg + 128 * tt:512 * tg + 128 * (tt + 1), :],
                    in_=rtk[tt % 2][:])), deps=[sBE], inc=16, tag=f"bd{tg}_{tt}")
                r_dma[(tg, tt)] = sBD
        res_last = r_dma[(7, 3)]

        p.barrier()
        p4b.close()
        p6 = ExitStack()
        h1t = [p6.enter_context(nc.sbuf_tensor(f"h1t{r}", [128, D], F32)) for r in range(2)]
        rest = [p6.enter_context(nc.sbuf_tensor(f"rest{r}", [128, D], BF16)) for r in range(2)]
        moet = [p6.enter_context(nc.sbuf_tensor(f"moet{r}", [128, D], BF16)) for r in range(2)]
        ln2t = p6.enter_context(nc.sbuf_tensor("ln2t", [128, D], F32))
        ft = p6.enter_context(nc.sbuf_tensor("ft", [128, D], F32))
        dmp6 = p6.enter_context(nc.sbuf_tensor("dmp6", [128, D], F32))
        scr6 = p6.enter_context(nc.sbuf_tensor("scr6", [128, D], F32))
        outt = [p6.enter_context(nc.sbuf_tensor(f"outt{r}", [128, D], F32)) for r in range(2)]
        st6 = p6.enter_context(nc.sbuf_tensor("st6", [128, 20], F32))

        f_dn = {}
        a6v = {}
        o6v = {}
        o_dn = {}
        for m in range(TPC):
            q6 = 10 * (m % 2)
            sLH = p.step("sync", lambda e, st, m=m: st.dma(e.dma_start(
                out=h1t[m % 2][:], in_=h1_d[128 * m:128 * (m + 1), :])),
                deps=[o_dn.get(m - 2)], inc=16, tag=f"lh{m}")
            sLR = p.step("sync", lambda e, st, m=m: st.dma(e.dma_start(
                out=rest[m % 2][:], in_=res_d[128 * m:128 * (m + 1), :])),
                deps=[res_last, f_dn.get(m - 2)], inc=16, tag=f"lr{m}")
            sZ6 = p.step("gpsimd", lambda e, _, m=m: e.memset(moet[m % 2][:], 0.0),
                         deps=[f_dn.get(m - 2)], tag=f"z6{m}")
            sG6 = p.step("gpsimd", lambda e, st, m=m: st.dma(e.indirect_dma_start(
                out=moet[m % 2][:], out_offset=None,
                in_=eout_d, in_offset=IndirectOffsetOnAxis(idxb[:, m:m + 1], 0),
                bounds_check=breg["r"], oob_is_err=False)),
                deps=[sZ6, eout_last], inc=16, tag=f"g6{m}")
            sE6 = p.step("scalar", lambda e, _, m=m: e.activation(
                out=ln2t[:], in_=h1t[m % 2][:], func=AF.Identity, bias=nmr2b[:, m:m + 1],
                scale=r2b[:, m:m + 1]), deps=[sLH, f_dn.get(m - 1)], tag=f"e6{m}")

            def f_f6(e, _, m=m, q6=q6):
                e.tensor_mul(st6[:, q6:q6 + 1], c1b[:, m:m + 1], g1b[:, m:m + 1])
                e.scalar_tensor_tensor(ft[:], rest[m % 2][:], c0b[:, m:m + 1], ln2t[:],
                                       op0=ALU.mult, op1=ALU.add)
                e.nop()
                return e.scalar_tensor_tensor(ft[:], moet[m % 2][:], st6[:, q6:q6 + 1],
                                              ft[:], op0=ALU.mult, op1=ALU.add)
            sF6 = p.step("vector", f_f6, deps=[sE6, sLR, sG6, a6v.get(m - 1)],
                         tag=f"f6{m}")
            f_dn[m] = sF6

            sFS = p.step("vector", lambda e, _, q6=q6: e.reduce_sum(
                st6[:, q6 + 1:q6 + 2], ft[:], axis=AX), deps=[sF6], tag=f"fs{m}")
            sFQ = p.step("scalar", lambda e, _, q6=q6: e.activation(
                out=dmp6[:], in_=ft[:], func=AF.Square, accum_out=st6[:, q6 + 2:q6 + 3]),
                deps=[sF6], tag=f"fq{m}")

            def f_v6(e, _, q6=q6):
                e.tensor_scalar(st6[:, q6 + 3:q6 + 4], st6[:, q6 + 1:q6 + 2], 1.0 / D,
                                None, op0=ALU.mult)
                e.tensor_scalar(st6[:, q6 + 4:q6 + 5], st6[:, q6 + 2:q6 + 3], 1.0 / D,
                                None, op0=ALU.mult)
                e.nop()
                e.tensor_mul(st6[:, q6 + 5:q6 + 6], st6[:, q6 + 3:q6 + 4],
                             st6[:, q6 + 3:q6 + 4])
                e.nop()
                return e.tensor_sub(st6[:, q6 + 6:q6 + 7], st6[:, q6 + 4:q6 + 5],
                                    st6[:, q6 + 5:q6 + 6])
            sV6 = p.step("vector", f_v6, deps=[sFQ], hard=[sFS], tag=f"v6{m}")

            def f_r6(e, _, q6=q6):
                e.activation(out=st6[:, q6 + 7:q6 + 8], in_=st6[:, q6 + 6:q6 + 7],
                             func=AF.Ln, bias=co(C_EPS5), scale=1.0)
                e.nop()
                return e.activation(out=st6[:, q6 + 7:q6 + 8], in_=st6[:, q6 + 7:q6 + 8],
                                    func=AF.Exp, bias=0.0, scale=co(C_NHALF))
            sR6 = p.step("scalar", f_r6, deps=[sV6], tag=f"r6{m}")
            def f_n6(e, _, q6=q6):
                e.tensor_copy(out=st6[:, q6 + 9:q6 + 10], in_=st6[:, q6 + 7:q6 + 8])
                return e.scalar_tensor_tensor(
                    st6[:, q6 + 8:q6 + 9], st6[:, q6 + 3:q6 + 4], -1.0,
                    st6[:, q6 + 7:q6 + 8], op0=ALU.mult, op1=ALU.mult)
            sN6 = p.step("vector", f_n6, deps=[sR6], tag=f"n6{m}")
            sA6 = p.step("scalar", lambda e, _, q6=q6: e.activation(
                out=scr6[:], in_=ft[:], func=AF.Identity, bias=st6[:, q6 + 8:q6 + 9],
                scale=st6[:, q6 + 9:q6 + 10]),
                deps=[sN6, o6v.get(m - 1)], tag=f"a6{m}")
            a6v[m] = sA6
            sO6 = p.step("vector", lambda e, _, m=m: e.tensor_add(
                outt[m % 2][:], h1t[m % 2][:], scr6[:]),
                deps=[sA6, o_dn.get(m - 2)], tag=f"o6{m}")
            o6v[m] = sO6
            sOD6 = p.step("sync", lambda e, st, m=m: st.dma(e.dma_start(
                out=out_d[128 * m:128 * (m + 1), :], in_=outt[m % 2][:])),
                deps=[sO6], inc=16, tag=f"od6{m}")
            o_dn[m] = sOD6

        p.barrier()
        p.step("vector", lambda e, _: e.nop() and None, deps=[o_dn[TPC - 1]], tag="fin")

        p.emit(blk)
        p6.close()
    return nc, taps


def make_inputs(inp):
    u = np.ascontiguousarray(np.asarray(inp["L0_attn_u"], dtype=np.float32)).reshape(256, 128, D)
    u_flat = u.reshape(256 * 128, D)
    iota = np.tile(np.arange(NE, dtype=np.float32), (128, 1))
    co = np.zeros((128, 16), dtype=np.float32)
    co[:, C_I1024] = 1.0 / D
    co[:, C_NHALF] = -0.5
    co[:, C_EPS5] = 1e-5
    co[:, C_EPS12] = 1e-12
    co[:, C_GC1] = 2 * 0.7978845608028654 * 0.044715
    co[:, C_GC2] = 2 * 0.7978845608028654
    com = {
        "WB": np.asarray(inp["L0_attn_WB"], dtype=np.float32),
        "WBf": np.asarray(inp["L0_attn_WBf"], dtype=np.float32),
        "w1": np.asarray(inp["L0_attn_w1"], dtype=np.float32).reshape(1, D),
        "w2": np.asarray(inp["L0_attn_w2"], dtype=np.float32).reshape(1, D),
        "bsc": np.asarray(inp["L0_attn_b"], dtype=np.float32).reshape(1, 1),
        "nwv": np.asarray(inp["L0_ffn_attn_nw"], dtype=np.float32).reshape(1, D),
        "nbv": np.asarray(inp["L0_ffn_attn_nb"], dtype=np.float32).reshape(1, D),
        "wg": np.asarray(inp["L0_ffn_wg"], dtype=np.float32),
        "rcoef": np.asarray(inp["L0_ffn_res_coef"], dtype=np.float32),
        "iw": np.asarray(inp["L0_ffn_inter_w"], dtype=np.float32),
        "ib": np.asarray(inp["L0_ffn_inter_b"], dtype=np.float32),
        "ow": np.asarray(inp["L0_ffn_output_w"], dtype=np.float32),
        "ob": np.asarray(inp["L0_ffn_output_b"], dtype=np.float32),
        "riw": np.asarray(inp["L0_ffn_res_inter_w"], dtype=np.float32),
        "rib": np.asarray(inp["L0_ffn_res_inter_b"], dtype=np.float32).reshape(1, 4096),
        "row_w": np.asarray(inp["L0_ffn_res_output_w"], dtype=np.float32),
        "rob": np.asarray(inp["L0_ffn_res_output_b"], dtype=np.float32).reshape(1, D),
        "ident_b": np.eye(128, dtype=ml_dtypes.bfloat16),
        "ident_f": np.eye(128, dtype=np.float32),
        "triu_b": np.triu(np.ones((128, 128))).astype(ml_dtypes.bfloat16),
        "striu32": np.triu(np.ones((32, 32), dtype=np.float32), k=1),
        "ident32": np.eye(32, dtype=np.float32),
        "ones1x128": np.ones((1, 128), dtype=np.float32),
        "ones128_b": np.ones((128, 1), dtype=ml_dtypes.bfloat16),
        "iota_in": iota, "consts_in": co,
        "oor_in": (OORV + np.arange(128, dtype=np.float32))[:, None],
    }
    cores = []
    for c in range(8):
        mine = u[c::8]
        halo = np.zeros((32, D), dtype=np.float32)
        for m in range(TPC):
            gt = 8 * m + c
            halo[m] = u_flat[gt * 128 + 127] if gt % 64 == 63 else u_flat[gt * 128 + 128]
        mf = np.zeros((128, 2), dtype=np.float32)
        mf[:, 1] = 1.0
        if c == 0:
            mf[0, 0] = 1.0
            mf[0, 1] = 0.0
        ml = np.zeros((128, 2), dtype=np.float32)
        ml[:, 1] = 1.0
        if c == 7:
            ml[127, 0] = 1.0
            ml[127, 1] = 0.0
        opv = np.zeros((8, 2), dtype=np.float32)
        opv[:, 0] = 1.0
        opv[:c, 1] = 1.0
        d = dict(com)
        d["u_t"] = np.ascontiguousarray(mine.reshape(TPC * 128, D))
        d["u_halo"] = halo
        d["mfirst"] = mf
        d["mlast"] = ml
        d["oneprev"] = opv
        cores.append(d)
    return cores


_cache = {}


def kernel(**inputs):
    if "nc" not in _cache:
        _cache["nc"] = build_nc()
    nc, taps = _cache["nc"]
    in_maps = make_inputs(inputs)
    res = run_bass_kernel_spmd(nc, in_maps, core_ids=list(range(8)))
    globals()["_last_results"] = res
    out = np.zeros((256, 128, D), dtype=np.float32)
    for c in range(8):
        out[c::8] = res.results[c]["out_d"].reshape(TPC, 128, D)
    return out.reshape(4, 8192, D)
